# version 01
# baseline (speedup 1.0000x reference)
import numpy as np
import ml_dtypes
import concourse.bass as bass
import concourse.mybir as mybir
from concourse.bass_utils import run_bass_kernel_spmd

F32 = mybir.dt.float32
BF16 = mybir.dt.bfloat16
AF = mybir.ActivationFunctionType
ALU = mybir.AluOpType
AX = mybir.AxisListType

T = 1024
D = 2048
NF = 16
HID = 5632
NH = 44
N_W = float(HID * D)


class Prog:
    def __init__(self):
        self.steps = []
        self.counts = {}
        self.events = {}

    def step(self, eng, emit, deps=(), event=None, inc=None, inc_amt=1):
        idx = len(self.steps)
        for d in deps:
            assert d in self.events, f"step {idx} ({event}) dep {d} not defined yet"
        self.steps.append(dict(eng=eng, emit=emit, deps=list(deps), event=event,
                               inc=inc, inc_amt=inc_amt, idx=idx))
        if inc is not None:
            v = self.counts.get(inc, 0) + inc_amt
            self.counts[inc] = v
            if event is not None:
                self.events[event] = (inc, v, eng, idx)

    def emit_engine(self, eng_name, e, sems, stop_idx=None):
        watermark = {}
        for st in self.steps:
            if stop_idx is not None and st["idx"] > stop_idx:
                continue
            if st["eng"] != eng_name:
                continue
            for dep in st["deps"]:
                sem_name, value, peng, pidx = self.events[dep]
                if watermark.get(sem_name, -1) >= value:
                    continue
                e.wait_ge(sems[sem_name], value)
                watermark[sem_name] = value
            instr = st["emit"](e)
            if st["inc"] is not None:
                assert instr is not None, f"step {st['idx']} on {eng_name} returned None"
                instr.then_inc(sems[st["inc"]], st["inc_amt"])


def build_nc(debug=False, stop_after=None):
    nc = bass.Bass("TRN2", target_bir_lowering=False, detect_race_conditions=False)
    pr = Prog()

    x_in = nc.dram_tensor("x_sh", [T, D], F32, kind="ExternalInput").ap()
    wq_in = nc.dram_tensor("wq_sh", [256, D], F32, kind="ExternalInput").ap()
    wkv_in = nc.dram_tensor("wkv_sh", [256, D], F32, kind="ExternalInput").ap()
    wo_in = nc.dram_tensor("wo_sh", [256, D], F32, kind="ExternalInput").ap()
    w1_h = nc.dram_tensor("w1_sh", [HID, 256], F32, kind="ExternalInput")
    w2_h = nc.dram_tensor("w2_sh", [HID, 256], F32, kind="ExternalInput")
    w1_in, w2_in = w1_h.ap(), w2_h.ap()
    w3_in = nc.dram_tensor("w3_sh", [256, HID], F32, kind="ExternalInput").ap()
    consts_in = nc.dram_tensor("consts", [128, 5 * 128], BF16, kind="ExternalInput").ap()
    onesc_in = nc.dram_tensor("onesc_i", [128, 1], BF16, kind="ExternalInput").ap()
    onesr_in = nc.dram_tensor("onesr_i", [1, 128], BF16, kind="ExternalInput").ap()
    mask_in = nc.dram_tensor("mask_i", [128, 10], F32, kind="ExternalInput").ap()

    y_out = nc.dram_tensor("y_sh", [T, D], F32, kind="ExternalOutput").ap()

    agin_g = nc.dram_tensor("agin_g", [128, 4], F32)
    agout_g = nc.dram_tensor("agout_g", [8 * 128, 4], F32)
    agin_attn = nc.dram_tensor("agin_attn", [3, 2, 128, D], BF16)
    agout_attn = nc.dram_tensor("agout_attn", [8, 3, 2, 128, D], BF16)
    agin_f12 = nc.dram_tensor("agin_f12", [NH, 128, 512], BF16)
    agout_f12 = nc.dram_tensor("agout_f12", [8, NH, 128, 512], BF16)
    agin_f3 = nc.dram_tensor("agin_f3", [2, 128, NH * 128], BF16)
    agout_f3 = nc.dram_tensor("agout_f3", [8, 2, 128, NH * 128], BF16)
    agin_sc = nc.dram_tensor("agin_sc", [128, 16], F32)
    agout_sc = nc.dram_tensor("agout_sc", [8 * 128, 16], F32)
    g_ph = nc.dram_tensor("g_ph", [128, 4], F32)
    g_thr2 = nc.dram_tensor("g_thr2", [1, 8], F32)

    dbg = {}
    if debug:
        for nm, shp in [("xn", [128, NF * T]), ("kcv", [128, NF * T]), ("qhs", [128, NF * T]),
                        ("sc", [128, 16]), ("carry", [128, 16]), ("thr", [128, 8]),
                        ("hid", [128, 22 * T]), ("qw", [128, 512])]:
            dbg[nm] = nc.dram_tensor("dbg_" + nm, shp, F32, kind="ExternalOutput").ap()

    import contextlib
    es = contextlib.ExitStack()
    with es:
        sb = lambda name, shape, dt: es.enter_context(nc.sbuf_tensor(name, shape, dt))
        ps = lambda name, shape, dt: es.enter_context(nc.psum_tensor(name, shape, dt))
        sm = lambda name: es.enter_context(nc.semaphore(name))
        c5 = sb("c5", [128, 5 * 128], BF16)
        onesc = sb("ones_c", [128, 1], BF16)
        onesr = sb("ones_r", [1, 128], BF16)
        mask = sb("mask", [128, 10], F32)
        xn = sb("xn", [128, NF * T], BF16)
        kcv = sb("kcv", [128, NF * T], BF16)
        qhs = sb("qhs", [128, NF * T], BF16)
        hid = sb("hid", [128, 22 * T], BF16)
        wstg = sb("wstg", [128, 2 * D], BF16)
        fstg = sb("fstg", [128, 8192], BF16)
        xarena = sb("xarena", [128, 2 * D], F32)
        wchunk = sb("wchunk", [128, 2 * 1024], F32)
        fa = sb("fa", [128, 1024], F32)
        small = sb("small", [128, 128], F32)
        rows = sb("rows", [1, 1024], F32)
        rowsb = sb("rowsb", [1, 1024], BF16)
        p0 = ps("p0", [128, 512], F32)
        p1 = ps("p1", [128, 512], F32)
        p2 = ps("p2", [128, 512], F32)
        p3 = ps("p3", [128, 512], F32)
        p4 = ps("p4", [128, 512], F32)
        p5 = ps("p5", [128, 512], F32)
        pt = ps("pt", [128, 512], F32)
        pt2 = ps("pt2", [128, 512], F32)
        stream_names = ["sK", "sx0", "sx1", "sa0", "sa1", "sg1", "sf0", "sf1",
                        "sj0", "sj1", "so2", "ss0", "ss1", "sb0", "sb1", "sw0", "sw1",
                        "sx2", "sy", "sc0", "sc1", "sd0", "sd1", "sp0", "sp1",
                        "gm", "gd"]
        sems = {n: sm(n) for n in stream_names}
        sems["f_v"] = sm("f_v")
        sems["s_ag"] = sm("s_ag")
        sems["t_v"] = sm("t_v")
        sems["t_s"] = sm("t_s")
        sems["t_t"] = sm("t_t")
        block = es.enter_context(nc.Block())
        cC, cSn, cCi, cSi, idn = (c5[:, i * 128:(i + 1) * 128] for i in range(5))
        PAIRS = [(p0, p1), (p2, p3)]

        def dma(eng, stream, out, in_, deps=(), event=None, slow=False):
            pr.step(eng, lambda e, o=out, i=in_, s=slow: e.dma_start(
                        out=o, in_=i, allow_slow_non_contiguous=s),
                    deps=deps, event=event or f"_dma{len(pr.steps)}", inc=stream, inc_amt=16)

        def vstep(emit, deps=(), event=None):
            pr.step("vector", emit, deps=deps, event=event, inc="t_v")

        def sstep(emit, deps=(), event=None):
            pr.step("scalar", emit, deps=deps, event=event, inc="t_s")

        def tstep(emit, deps=(), event=None):
            pr.step("tensor", emit, deps=deps, event=event, inc="t_t")

        def ag_step(name, ain, aout, deps):
            pr.step("gpsimd",
                    lambda e, ain=ain, aout=aout: e.collective_compute(
                        "AllGather", ALU.bypass, replica_groups=[list(range(8))],
                        ins=[ain.ap()], outs=[aout.ap()]),
                    deps=deps, event=name, inc="s_ag")

        fcnt = {"n": 0}

        def vfence(e, instr):
            fcnt["n"] += 1
            instr.then_inc(sems["f_v"], 1)
            e.wait_ge(sems["f_v"], fcnt["n"])

        tp_banks = [pt, pt2]
        tp_state = {"last_evac": [None, None], "n": 0}

        def transpose_tiles(pairs, tag, first_tdeps=(), vemit=None, vdeps_extra=None):
            evs = []
            ngrp = (len(pairs) + 3) // 4
            for g in range(ngrp):
                grp = pairs[g * 4:(g + 1) * 4]
                b = tp_state["n"] % 2
                tp_state["n"] += 1
                bank = tp_banks[b]
                tdeps = list(first_tdeps) if g <= 1 else []
                if tp_state["last_evac"][b] is not None:
                    tdeps.append(tp_state["last_evac"][b])

                def em(e, grp=grp, bank=bank):
                    for k, (srcp, _) in enumerate(grp):
                        e.matmul(out=bank[:, k * 128:(k + 1) * 128], lhsT=srcp,
                                 rhs=idn, start=(k == 0), stop=(k == len(grp) - 1))
                    return e.drain()
                tev = f"tp_{tag}_{g}_t"
                pr.step("tensor", em, deps=tdeps, event=tev, inc="t_t")

                def vem(e, grp=grp, bank=bank, vemit=vemit):
                    ins = None
                    for k, (_, dstp) in enumerate(grp):
                        if vemit is None:
                            ins = e.tensor_copy(out=dstp, in_=bank[:, k * 128:(k + 1) * 128])
                        else:
                            ins = vemit(e, bank[:, k * 128:(k + 1) * 128], dstp)
                    return ins
                vev = f"tp_{tag}_{g}_v"
                vdeps = [tev] + (list(vdeps_extra[g]) if vdeps_extra else [])
                pr.step("vector", vem, deps=vdeps, event=vev, inc="t_v")
                tp_state["last_evac"][b] = vev
                evs.append(vev)
            return evs

        dma("sync", "sK", c5[:, :], consts_in, event="ld_c5")
        dma("sync", "sK", onesc[:, :], onesc_in, event="ld_onesc")
        dma("sync", "sK", onesr[:1, :], onesr_in, event="ld_onesr")
        dma("sync", "sK", mask[:, :], mask_in, event="ld_mask")
        for _nm in ("ld_c5", "ld_onesc", "ld_onesr", "ld_mask"):
            pr.events[_nm] = pr.events["ld_mask"]

        for i in range(8):
            xs = xarena[:, (i % 2) * 2048:(i % 2 + 1) * 2048]
            dma("sync", f"sx{i % 2}", xs, x_in[i * 128:(i + 1) * 128, :],
                deps=[f"ln_xn{i-2}"] if i >= 2 else [], event=f"ln_ld{i}")
            vstep(lambda e, xs=xs: e.tensor_reduce(out=small[:, 16:17], in_=xs, axis=AX.X,
                                                   op=ALU.add),
                  deps=[f"ln_ld{i}"] + ([f"ln_var{i-1}"] if i > 0 else []),
                  event=f"ln_r1{i}")
            for h in range(2):
                sq_deps = [f"ln_ld{i}"]
                if h > 0:
                    sq_deps.append(f"ln_r2{i}_0")
                elif i > 0:
                    sq_deps.append(f"ln_r2{i-1}_1")
                sstep(lambda e, xs=xs, h=h: e.activation(out=fa[:, 0:1024],
                                                         in_=xs[:, h * 1024:(h + 1) * 1024],
                                                         func=AF.Square),
                      deps=sq_deps, event=f"ln_sq{i}_{h}")
                vstep(lambda e, h=h: e.tensor_reduce(out=small[:, 18 + h:19 + h],
                                                     in_=fa[:, 0:1024], axis=AX.X, op=ALU.add),
                      deps=[f"ln_sq{i}_{h}"], event=f"ln_r2{i}_{h}")

            def em(e):
                vfence(e, e.tensor_scalar(out=small[:, 16:17], in0=small[:, 16:17],
                                          scalar1=1.0 / 2048, scalar2=None, op0=ALU.mult))
                vfence(e, e.tensor_tensor(out=small[:, 18:19], in0=small[:, 18:19],
                                          in1=small[:, 19:20], op=ALU.add))
                vfence(e, e.tensor_tensor(out=small[:, 20:21], in0=small[:, 16:17],
                                          in1=small[:, 16:17], op=ALU.mult))
                return e.scalar_tensor_tensor(out=small[:, 18:19], in0=small[:, 18:19],
                                              scalar=1.0 / 2048, in1=small[:, 20:21],
                                              op0=ALU.mult, op1=ALU.subtract)
            vstep(em, deps=[f"ln_r1{i}", f"ln_r2{i}_0", f"ln_r2{i}_1"], event=f"ln_var{i}")
            sstep(lambda e: e.activation(out=small[:, 21:22], in_=small[:, 18:19],
                                         func=AF.Sqrt, bias=mask[:, 8:9]),
                  deps=[f"ln_var{i}", "ld_mask"], event=f"ln_sd{i}")

            def em(e, xs=xs):
                vfence(e, e.reciprocal(out=small[:, 21:22], in_=small[:, 21:22]))
                return e.tensor_scalar(out=fstg[:, 6144:8192], in0=xs, scalar1=small[:, 16:17],
                                       scalar2=small[:, 21:22], op0=ALU.subtract, op1=ALU.mult)
            vstep(em, deps=[f"ln_sd{i}"] + ([f"ln_tr{i-1}_1"] if i > 0 else []),
                  event=f"ln_xn{i}")
            pairs = [(fstg[:, 6144 + kt * 128:6144 + (kt + 1) * 128],
                      xn[:, kt * T + i * 128:kt * T + (i + 1) * 128]) for kt in range(16)]
            evs = transpose_tiles(pairs, f"ln{i}", first_tdeps=["ld_c5", f"ln_xn{i}"])
            pr.events[f"ln_ev{i}_0"] = pr.events[evs[1]]
            pr.events[f"ln_ev{i}_1"] = pr.events[evs[3]]
            pr.events[f"ln_tr{i}_1"] = pr.events[f"tp_ln{i}_3_t"]
        xn_all = [f"ln_ev{i}_{h}" for i in range(8) for h in range(2)]
        if debug:
            dma("gpsimd", "gd", dbg["xn"], xn[:, :], deps=xn_all, event="dbg_xn")

        natA = [kcv[:, 0:2048], kcv[:, 2048:4096]]
        strA = [kcv[:, 4096:6144], kcv[:, 6144:8192]]
        attn_list = [(0, wkv_in), (1, wq_in), (2, wo_in)]
        for j in range(6):
            m_i, src = attn_list[j // 2]
            mtloc = j % 2
            dma("gpsimd", f"sp{j % 2}", natA[j % 2], src[mtloc * 128:(mtloc + 1) * 128, :],
                deps=[f"pA_tr{j-2}_1"] if j >= 2 else [], event=f"pA_ld{j}")
            pairs = [(natA[j % 2][:, kt * 128:(kt + 1) * 128],
                      strA[j % 2][:, kt * 128:(kt + 1) * 128]) for kt in range(16)]
            evs = transpose_tiles(pairs, f"pA{j}", first_tdeps=["ld_c5", f"pA_ld{j}"],
                                  vdeps_extra=[([f"pA_out{j-2}"] if j >= 2 and k == 0 else [])
                                               for k in range(4)])
            pr.events[f"pA_tr{j}_1"] = pr.events[f"tp_pA{j}_3_t"]
            dma("sync", f"sa{j % 2}", agin_attn[m_i, mtloc], strA[j % 2][:, :],
                deps=[evs[3]], event=f"pA_out{j}")
        ag_step("AG_attn", agin_attn, agout_attn, [f"pA_out{j}" for j in range(6)])

        gsrc = []
        for m_i, wh in enumerate([w1_h, w2_h]):
            for ck in range(11):
                gsrc.append((m_i, bass.AP(wh, ck * 131072, [[256, 128], [32768, 4], [1, 256]]),
                             1024))
        for mtloc in range(2):
            for ch in range(6):
                w_ = 1024 if ch < 5 else 512
                gsrc.append((2, w3_in[mtloc * 128:(mtloc + 1) * 128, ch * 1024:ch * 1024 + w_], w_))

        vstep(lambda e: e.memset(small[:, 0:4], 0.0), event="g_zero")
        for i, (m_i, src, w_) in enumerate(gsrc):
            slot = wchunk[:, (i % 2) * 1024:(i % 2) * 1024 + w_]
            dma("scalar", f"sc{i % 2}", slot, src,
                deps=[f"g_red{i-2}"] if i >= 2 else [], event=f"g_ld{i}")

            def em(e, slot=slot, m_i=m_i):
                vfence(e, e.tensor_reduce(out=small[:, 12:13], in_=slot, axis=AX.X, op=ALU.add,
                                          apply_absolute_value=True))
                return e.tensor_tensor(out=small[:, m_i:m_i + 1], in0=small[:, m_i:m_i + 1],
                                       in1=small[:, 12:13], op=ALU.add)
            vstep(em, deps=["g_zero", f"g_ld{i}"], event=f"g_red{i}")
        g_last = f"g_red{len(gsrc)-1}"
        dma("sync", "sg1", agin_g[:, :], small[:, 0:4], deps=[g_last], event="g_aginw")
        ag_step("AG_g", agin_g, agout_g, ["g_aginw"])
        dma("gpsimd", "gm", small[:, 32:64], bass.AP(agout_g, 0, [[4, 128], [512, 8], [1, 4]]),
            deps=["AG_g"], event="g_gath")

        def em(e):
            vfence(e, e.tensor_tensor(out=small[:, 32:48], in0=small[:, 32:48],
                                      in1=small[:, 48:64], op=ALU.add))
            vfence(e, e.tensor_tensor(out=small[:, 32:40], in0=small[:, 32:40],
                                      in1=small[:, 40:48], op=ALU.add))
            return e.tensor_tensor(out=small[:, 32:36], in0=small[:, 32:36],
                                   in1=small[:, 36:40], op=ALU.add)
        vstep(em, deps=["g_gath"], event="g_sum")
        dma("gpsimd", "gm", g_ph[:, :], small[:, 32:36], deps=["g_sum"], event="g_phw")
        dma("gpsimd", "gm", rows[:1, 0:512], bass.AP(g_ph, 0, [[512, 1], [1, 4], [4, 128]]),
            deps=["g_phw"], event="g_phr", slow=True)

        def em(e):
            src = bass.AP(rows, 0, [[1024, 1], [128, 4], [1, 128]])
            vfence(e, e.tensor_reduce(out=rows[:1, 512:516], in_=src, axis=AX.X, op=ALU.add))
            vfence(e, e.tensor_scalar(out=rows[:1, 520:524], in0=rows[:1, 512:516],
                                      scalar1=0.5 / N_W, scalar2=0.5e-8, op0=ALU.mult,
                                      op1=ALU.add))
            return e.tensor_scalar(out=rows[:1, 524:528], in0=rows[:1, 520:524],
                                   scalar1=-1.0, scalar2=None, op0=ALU.mult)
        vstep(em, deps=["g_phr"], event="g_thr")
        dma("gpsimd", "gm", g_thr2[:1, 0:4], rows[:1, 520:524], deps=["g_thr"], event="g_tw1")
        dma("gpsimd", "gm", g_thr2[:1, 4:8], rows[:1, 524:528], deps=["g_tw1"], event="g_tw2")
        dma("gpsimd", "gm", small[:, 4:12], bass.AP(g_thr2, 0, [[0, 128], [1, 8]]),
            deps=["g_tw2"], event="g_thrb")
        thr = lambda j: small[:, 4 + j:5 + j]
        thrn = lambda j: small[:, 8 + j:9 + j]
        if debug:
            dma("gpsimd", "gd", dbg["thr"], small[:, 4:12], deps=["g_thrb"], event="dbg_thr")

        qbA = [qhs[:, 0:512], qhs[:, 512:1024]]
        stF = [qhs[:, 1024:1536], qhs[:, 1536:2048]]
        mskb = qhs[:, 2048:3072]
        for grp in range(11):
            for m_i, wh in enumerate([w1_h, w2_h]):
                dma("scalar", f"sc{m_i}", wchunk[:, m_i * 1024:(m_i + 1) * 1024],
                    bass.AP(wh, grp * 131072, [[256, 128], [32768, 4], [1, 256]]),
                    deps=[f"q12_qz{grp-1}_3_1"] if grp >= 1 else [g_last],
                    event=f"q12_ld{grp}_{m_i}")
            for a in range(4):
                ht = grp * 4 + a
                for m_i in range(2):
                    src = wchunk[:, m_i * 1024 + a * 256:m_i * 1024 + (a + 1) * 256]

                    def em(e, src=src, m_i=m_i, ht=ht):
                        e.tensor_scalar(out=mskb[:, 0:256], in0=src, scalar1=thrn(m_i),
                                        scalar2=None, op0=ALU.is_le)
                        return e.scalar_tensor_tensor(out=qbA[ht % 2][:, m_i * 256:(m_i + 1) * 256],
                                                      in0=src, scalar=thr(m_i), in1=mskb[:, 0:256],
                                                      op0=ALU.is_ge, op1=ALU.subtract)
                    vstep(em, deps=["g_thrb", f"q12_ld{grp}_{m_i}"] +
                          ([f"q12_tr{ht-2}"] if ht >= 2 else []),
                          event=f"q12_qz{grp}_{a}_{m_i}")

                pairs = [(qbA[ht % 2][:, tl * 128:(tl + 1) * 128],
                          stF[ht % 2][:, tl * 128:(tl + 1) * 128]) for tl in range(4)]
                evs = transpose_tiles(pairs, f"q12t{ht}",
                                      first_tdeps=[f"q12_qz{grp}_{a}_0", f"q12_qz{grp}_{a}_1"],
                                      vdeps_extra=[[f"q12_out{ht-2}"] if ht >= 2 else []])
                pr.events[f"q12_tr{ht}"] = pr.events[f"tp_q12t{ht}_0_t"]
                pr.events[f"q12_ev{ht}"] = pr.events[evs[0]]
                dma("sync", f"sf{ht % 2}", agin_f12[ht], stF[ht % 2][:, :],
                    deps=[f"q12_ev{ht}"], event=f"q12_out{ht}")
        ag_step("AG_f12", agin_f12, agout_f12, [f"q12_out{NH-1}", f"q12_out{NH-2}"])
        if debug:
            dma("gpsimd", "gd", dbg["qw"], agout_f12[0, 0], deps=["AG_f12"], event="dbg_qw")

        for mtloc in range(2):
            for ch in range(6):
                i3 = mtloc * 6 + ch
                w_ = 1024 if ch < 5 else 512
                ntl = w_ // 128
                dma("scalar", f"sc{i3 % 2}", wchunk[:, (i3 % 2) * 1024:(i3 % 2) * 1024 + w_],
                    w3_in[mtloc * 128:(mtloc + 1) * 128, ch * 1024:ch * 1024 + w_],
                    deps=[f"q3_qz{i3-2}"] if i3 >= 2 else ["q12_qz10_3_1", "q12_qz10_3_0"],
                    event=f"q3_ld{i3}")

                def em(e, i3=i3, w_=w_):
                    src = wchunk[:, (i3 % 2) * 1024:(i3 % 2) * 1024 + w_]
                    e.tensor_scalar(out=mskb[:, 0:w_], in0=src, scalar1=thrn(2),
                                    scalar2=None, op0=ALU.is_le)
                    return e.scalar_tensor_tensor(out=qhs[:, 3072:3072 + w_], in0=src,
                                                  scalar=thr(2), in1=mskb[:, 0:w_],
                                                  op0=ALU.is_ge, op1=ALU.subtract)
                vstep(em, deps=["g_thrb", f"q3_ld{i3}"] + ([f"q3_tr{i3-1}"] if i3 >= 1 else []),
                      event=f"q3_qz{i3}")

                pairs = [(qhs[:, 3072 + tl * 128:3072 + (tl + 1) * 128],
                          fstg[:, ch * 1024 + tl * 128:ch * 1024 + (tl + 1) * 128])
                         for tl in range(ntl)]
                ngrp3 = (ntl + 3) // 4
                evs = transpose_tiles(pairs, f"q3t{i3}", first_tdeps=[f"q3_qz{i3}"],
                                      vdeps_extra=[([f"q3_out{mtloc-1}"]
                                                    if ch == 0 and mtloc >= 1 and g == 0 else [])
                                                   for g in range(ngrp3)])
                pr.events[f"q3_tr{i3}"] = pr.events[f"tp_q3t{i3}_{ngrp3-1}_t"]
                pr.events[f"q3_ev{i3}"] = pr.events[evs[-1]]
            dma("sync", f"sj{mtloc}", agin_f3[mtloc], fstg[:, 0:5632],
                deps=[f"q3_ev{mtloc * 6 + 5}", f"q3_ev{mtloc * 6 + 4}"], event=f"q3_out{mtloc}")
        ag_step("AG_f3", agin_f3, agout_f3, ["q3_out0", "q3_out1"])

        sq = fstg[:, 6144:7168]
        rbs = fstg[:, 7168:8192]

        def hname(hh):
            return ("kv" if hh < 16 else "qq") + str(hh % 16)

        def cons_events(hh):
            nm = hname(hh)
            if hh < 16:
                return [f"{nm}_kh0", f"{nm}_kh1"]
            return [f"{nm}_qh0", f"{nm}_qh1"]

        for hh in range(32):
            mt = hh % 16
            mat = 0 if hh < 16 else 1
            nm = hname(hh)
            A, B = PAIRS[hh % 2]
            NA, NB = PAIRS[(hh + 1) % 2]
            stripe = wstg[:, (hh % 2) * 2048:(hh % 2 + 1) * 2048]

            dma("sync", f"ss{hh % 2}", stripe, agout_attn[mt // 2, mat, mt % 2],
                deps=["AG_attn"] + ([f"{hname(hh-2)}_mm"] if hh >= 2 else []),
                event=f"{nm}_w")

            def em(e, stripe=stripe, A=A, B=B):
                for kt in range(16):
                    e.matmul(out=A[:, :], lhsT=stripe[:, kt * 128:(kt + 1) * 128],
                             rhs=xn[:, kt * T:kt * T + 512], start=(kt == 0), stop=(kt == 15))
                for kt in range(16):
                    e.matmul(out=B[:, :], lhsT=stripe[:, kt * 128:(kt + 1) * 128],
                             rhs=xn[:, kt * T + 512:kt * T + 1024], start=(kt == 0),
                             stop=(kt == 15))
                return e.drain()
            mm_deps = [f"{nm}_w"] + (xn_all if hh < 2 else [])
            if hh >= 2:
                mm_deps += cons_events(hh - 2)
            if hh >= 1:
                mm_deps += [f"{hname(hh-1)}_rsA", f"{hname(hh-1)}_rsB"]
            tstep(em, deps=mm_deps, event=f"{nm}_mm")

            for h, ps in ((0, A), (1, B)):
                sstep(lambda e, ps=ps, h=h: e.activation(out=sq[:, h * 512:(h + 1) * 512],
                                                         in_=ps[:, :], func=AF.Square),
                      deps=[f"{nm}_mm"], event=f"{nm}_sq{h}")

            def em(e, NA=NA, NB=NB):
                e.matmul(out=NA[0:1, :], lhsT=onesc[:, 0:1], rhs=sq[:, 0:512],
                         start=True, stop=True)
                e.matmul(out=NB[0:1, :], lhsT=onesc[:, 0:1], rhs=sq[:, 512:1024],
                         start=True, stop=True)
                return e.drain()
            ns_deps = [f"{nm}_sq0", f"{nm}_sq1", "ld_onesc"]
            if hh >= 1:
                ns_deps += cons_events(hh - 1)
            tstep(em, deps=ns_deps, event=f"{nm}_ns")

            sstep(lambda e, NA=NA: e.activation(out=rows[:1, 0:512], in_=NA[0:1, :],
                                                func=AF.Sqrt, scale=128.0),
                  deps=[f"{nm}_ns"], event=f"{nm}_rsA")
            sstep(lambda e, NB=NB: e.activation(out=rows[:1, 512:1024], in_=NB[0:1, :],
                                                func=AF.Sqrt, scale=128.0),
                  deps=[f"{nm}_ns"], event=f"{nm}_rsB")

            def em(e):
                vfence(e, e.reciprocal(out=rows[:1, 0:1024], in_=rows[:1, 0:1024]))
                return e.tensor_copy(out=rowsb[:1, 0:1024], in_=rows[:1, 0:1024])
            vstep(em, deps=[f"{nm}_rsA", f"{nm}_rsB"] + (["g_tw1", "g_tw2"] if hh == 0 else []),
                  event=f"{nm}_ri")

            def em(e):
                e.matmul(out=p4[:, :], lhsT=onesr[:1, :], rhs=rowsb[:1, 0:512],
                         start=True, stop=True)
                e.matmul(out=p5[:, :], lhsT=onesr[:1, :], rhs=rowsb[:1, 512:1024],
                         start=True, stop=True)
                return e.drain()
            rb_deps = [f"{nm}_ri", "ld_onesr"]
            if hh >= 1:
                rb_deps.append(f"{hname(hh-1)}_rbe1")
            tstep(em, deps=rb_deps, event=f"{nm}_rb")

            rbe_deps0 = [f"{nm}_rb"]
            rbe_deps1 = [f"{nm}_rb"]
            if hh >= 1:
                rbe_deps0 += cons_events(hh - 1)
            sstep(lambda e: e.copy(out=rbs[:, 0:512], in_=p4[:, :]),
                  deps=rbe_deps0, event=f"{nm}_rbe0")
            sstep(lambda e: e.copy(out=rbs[:, 512:1024], in_=p5[:, :]),
                  deps=rbe_deps1, event=f"{nm}_rbe1")

            if hh < 16:
                for h, ps in ((0, A), (1, B)):
                    vstep(lambda e, ps=ps, h=h: e.tensor_tensor(
                              out=fa[:, h * 512:(h + 1) * 512], in0=ps[:, :],
                              in1=rbs[:, h * 512:(h + 1) * 512], op=ALU.mult),
                          deps=[f"{nm}_rbe{h}"], event=f"{nm}_kh{h}")

                def em(e, mt=mt):
                    vfence(e, e.tensor_tensor_scan(out=kcv[:, mt * T:(mt + 1) * T], data0=fa[:, :],
                                                   data1=fa[:, :], initial=0.0, op0=ALU.add,
                                                   op1=ALU.bypass))
                    return e.tensor_copy(out=small[:, 24 + mt:25 + mt],
                                        in_=kcv[:, mt * T + 1023:mt * T + 1024])
                vstep(em, deps=[f"{nm}_kh0", f"{nm}_kh1"], event=f"{nm}_scan")
            else:
                for h, ps in ((0, A), (1, B)):
                    vstep(lambda e, ps=ps, h=h, mt=mt: e.tensor_tensor(
                              out=qhs[:, mt * T + h * 512:mt * T + (h + 1) * 512],
                              in0=ps[:, :], in1=rbs[:, h * 512:(h + 1) * 512], op=ALU.mult),
                          deps=[f"{nm}_rbe{h}"], event=f"{nm}_qh{h}")
        sc_all = [f"kv{mt}_scan" for mt in range(16)]
        if debug:
            dma("gpsimd", "gd", dbg["kcv"], kcv[:, :], deps=sc_all, event="dbg_kcv")
            dma("gpsimd", "gd", dbg["sc"], small[:, 24:40], deps=sc_all, event="dbg_sc")

        dma("sync", "so2", agin_sc[:, :], small[:, 24:40], deps=sc_all, event="sc_w")
        ag_step("AG_sc", agin_sc, agout_sc, ["sc_w"])
        dma("gpsimd", "gm", fa[:, 0:128],
            bass.AP(agout_sc, 0, [[16, 128], [2048, 8], [1, 16]]),
            deps=["AG_sc", "kv15_scan", "kv14_scan"], event="sc_gath")

        def em(e):
            vfence(e, e.tensor_scalar(out=small[:, 64:80], in0=fa[:, 0:16], scalar1=mask[:, 0:1],
                                      scalar2=None, op0=ALU.mult))
            for r in range(1, 8):
                src_ = small[:, 64 + 16 * ((r + 1) % 2):80 + 16 * ((r + 1) % 2)]
                dst_ = small[:, 64 + 16 * (r % 2):80 + 16 * (r % 2)]
                vfence(e, e.scalar_tensor_tensor(out=dst_, in0=fa[:, r * 16:(r + 1) * 16],
                                                 scalar=mask[:, r:r + 1], in1=src_,
                                                 op0=ALU.mult, op1=ALU.add))
            return e.engine_nop()
        vstep(em, deps=["sc_gath", "ld_mask"], event="carry")
        carry = small[:, 80:96]
        if debug:
            dma("gpsimd", "gd", dbg["carry"], carry, deps=["carry"], event="dbg_carry")

        qfre_b = wstg[:, 0:512]
        qfim_b = wstg[:, 512:1024]
        kfre_b = wstg[:, 1024:1536]
        kfim_b = wstg[:, 1536:2048]
        t1_b = wstg[:, 2048:2560]
        t2_b = wstg[:, 2560:3072]
        t4_b = wstg[:, 3072:3584]
        prev = {}
        for mt in range(16):
            for h in range(2):
                kin_deps = ["carry", f"kv{mt}_scan"]
                if ("fft", h) in prev:
                    kin_deps.append(prev[("fft", h)])
                vstep(lambda e, mt=mt, h=h: e.tensor_scalar(
                          out=fstg[:, h * 512:(h + 1) * 512],
                          in0=kcv[:, mt * T + h * 512:mt * T + (h + 1) * 512],
                          scalar1=carry[:, mt:mt + 1], scalar2=None, op0=ALU.add),
                      deps=kin_deps, event=f"v{mt}_{h}_kin")

                def em(e, mt=mt, h=h):
                    sl = slice(mt * T + h * 512, mt * T + (h + 1) * 512)
                    e.matmul(out=p0[:, :], lhsT=cC, rhs=qhs[:, sl], start=True, stop=True)
                    e.matmul(out=p1[:, :], lhsT=cSn, rhs=qhs[:, sl], start=True, stop=True)
                    e.matmul(out=p2[:, :], lhsT=cC, rhs=fstg[:, h * 512:(h + 1) * 512],
                             start=True, stop=True)
                    e.matmul(out=p3[:, :], lhsT=cSn, rhs=fstg[:, h * 512:(h + 1) * 512],
                             start=True, stop=True)
                    return e.drain()
                fft_deps = [f"v{mt}_{h}_kin", f"qq{mt}_qh{h}"]
                if "pev" in prev:
                    fft_deps.append(prev["pev"])
                else:
                    fft_deps += ["qq15_qh0", "qq15_qh1", "qq14_qh0", "qq14_qh1"]
                tstep(em, deps=fft_deps, event=f"v{mt}_{h}_fft")

                def em(e):
                    e.copy(out=qfre_b, in_=p0[:, :])
                    e.copy(out=qfim_b, in_=p1[:, :])
                    e.copy(out=kfre_b, in_=p2[:, :])
                    return e.copy(out=kfim_b, in_=p3[:, :])
                ev_deps = [f"v{mt}_{h}_fft"]
                if "cplx" in prev:
                    ev_deps.append(prev["cplx"])
                sstep(em, deps=ev_deps, event=f"v{mt}_{h}_pev")

                def em(e):
                    e.tensor_tensor(out=t1_b, in0=qfre_b, in1=kfre_b, op=ALU.mult)
                    e.tensor_tensor(out=t2_b, in0=qfim_b, in1=kfim_b, op=ALU.mult)
                    e.tensor_tensor(out=t1_b, in0=t1_b, in1=t2_b, op=ALU.subtract)
                    e.tensor_tensor(out=t2_b, in0=qfre_b, in1=kfim_b, op=ALU.mult)
                    e.tensor_tensor(out=t4_b, in0=qfim_b, in1=kfre_b, op=ALU.mult)
                    return e.tensor_tensor(out=t2_b, in0=t2_b, in1=t4_b, op=ALU.add)
                cx_deps = [f"v{mt}_{h}_pev"]
                if "vmm" in prev:
                    cx_deps.append(prev["vmm"])
                vstep(em, deps=cx_deps, event=f"v{mt}_{h}_cplx")

                def em(e):
                    e.matmul(out=p4[:, :], lhsT=cCi, rhs=t1_b, start=True, stop=False)
                    e.matmul(out=p4[:, :], lhsT=cSi, rhs=t2_b, start=False, stop=True)
                    return e.drain()
                vm_deps = [f"v{mt}_{h}_cplx"]
                if "vev" in prev:
                    vm_deps.append(prev["vev"])
                else:
                    vm_deps += ["qq15_rbe0", "qq15_rbe1"]
                    if debug:
                        vm_deps += ["dbg_kcv", "dbg_sc"]
                tstep(em, deps=vm_deps, event=f"v{mt}_{h}_vmm")
                sstep(lambda e, mt=mt, h=h: e.copy(
                          out=kcv[:, mt * T + h * 512:mt * T + (h + 1) * 512], in_=p4[:, :]),
                      deps=[f"v{mt}_{h}_vmm"], event=f"v{mt}_{h}_vev")
                prev[("fft", h)] = f"v{mt}_{h}_fft"
                prev["pev"] = f"v{mt}_{h}_pev"
                prev["cplx"] = f"v{mt}_{h}_cplx"
                prev["vmm"] = f"v{mt}_{h}_vmm"
                prev["vev"] = f"v{mt}_{h}_vev"
        val_all = [f"v{mt}_{h}_vev" for mt in range(16) for h in range(2)]

        for mt in range(16):
            A, B = PAIRS[mt % 2]
            stripe = wstg[:, (mt % 2) * 2048:(mt % 2 + 1) * 2048]
            st_deps = ["AG_attn", "v15_1_cplx", "v15_1_vmm"]
            if mt >= 2:
                st_deps.append(f"s{mt-2}_mm")
            dma("sync", f"ss{mt % 2}", stripe, agout_attn[mt // 2, 2, mt % 2],
                deps=st_deps, event=f"s{mt}_w")

            def em(e, stripe=stripe, A=A, B=B):
                for kt in range(16):
                    e.matmul(out=A[:, :], lhsT=stripe[:, kt * 128:(kt + 1) * 128],
                             rhs=kcv[:, kt * T:kt * T + 512], start=(kt == 0), stop=(kt == 15))
                for kt in range(16):
                    e.matmul(out=B[:, :], lhsT=stripe[:, kt * 128:(kt + 1) * 128],
                             rhs=kcv[:, kt * T + 512:kt * T + 1024], start=(kt == 0),
                             stop=(kt == 15))
                return e.drain()
            mm_deps = [f"s{mt}_w"]
            if mt == 0:
                mm_deps += val_all + ["v15_1_fft", "v15_0_fft"]
            if mt >= 2:
                mm_deps += [f"s{mt-2}_ev0", f"s{mt-2}_ev1"]
            tstep(em, deps=mm_deps, event=f"s{mt}_mm")
            for h, ps in ((0, A), (1, B)):
                sstep(lambda e, ps=ps, mt=mt, h=h: e.copy(
                          out=qhs[:, mt * T + h * 512:mt * T + (h + 1) * 512], in_=ps[:, :]),
                      deps=[f"s{mt}_mm"], event=f"s{mt}_ev{h}")
        s_all = [f"s{mt}_ev{h}" for mt in range(16) for h in range(2)]
        if debug:
            dma("gpsimd", "gd", dbg["qhs"], qhs[:, :], deps=s_all, event="dbg_qhs")

        prevf = {}
        for g10 in range(2):
            for lht in range(22):
                ht = g10 * 22 + lht
                slot = (lht % 2) * 4096
                for r in range(8):
                    dstap = bass.AP(fstg, slot + 256 * r,
                                    [[8192, 128], [2048, 2], [128, 2], [1, 128]])
                    q, qs = ("scalar", f"sd{lht % 2}") if r < 4 else ("sync", f"sb{lht % 2}")
                    deps = ["AG_f12"]
                    if g10 == 0 and lht == 0:
                        deps += ["v15_1_fft"]
                    if g10 == 0 and lht == 1:
                        deps += ["qq15_qh0", "qq15_qh1", "v15_1_fft"]
                    if lht >= 2:
                        deps.append(f"f{g10}_{lht-2}_x2")
                    if g10 == 1 and lht < 2:
                        deps += ["o0_15_mm", "o0_14_mm", f"f0_{20+lht}_x2"]
                        if debug:
                            deps += ["dbg_hid"]
                    dma(q, qs, dstap, agout_f12[r, ht], deps=deps, event=f"f{g10}_{lht}_w{r}")
                wdeps = [f"f{g10}_{lht}_w{r}" for r in range(8)]

                def em(e, slot=slot):
                    for kt in range(16):
                        e.matmul(out=p0[:, :], lhsT=fstg[:, slot + kt * 128:slot + (kt + 1) * 128],
                                 rhs=xn[:, kt * T:kt * T + 512], start=(kt == 0), stop=(kt == 15))
                    for kt in range(16):
                        e.matmul(out=p1[:, :], lhsT=fstg[:, slot + kt * 128:slot + (kt + 1) * 128],
                                 rhs=xn[:, kt * T + 512:kt * T + 1024], start=(kt == 0),
                                 stop=(kt == 15))
                    return e.drain()
                x1_deps = wdeps[:]
                if (g10, lht) == (0, 0):
                    x1_deps += ["s15_ev0", "s15_ev1", "s14_ev0", "s14_ev1"]
                else:
                    x1_deps += [prevf["sig0"], prevf["sig1"]]
                tstep(em, deps=x1_deps, event=f"f{g10}_{lht}_x1")

                def em(e, slot=slot):
                    off = slot + 2048
                    for kt in range(16):
                        e.matmul(out=p2[:, :], lhsT=fstg[:, off + kt * 128:off + (kt + 1) * 128],
                                 rhs=xn[:, kt * T:kt * T + 512], start=(kt == 0), stop=(kt == 15))
                    for kt in range(16):
                        e.matmul(out=p3[:, :], lhsT=fstg[:, off + kt * 128:off + (kt + 1) * 128],
                                 rhs=xn[:, kt * T + 512:kt * T + 1024], start=(kt == 0),
                                 stop=(kt == 15))
                    return e.drain()
                x2_deps = []
                if (g10, lht) == (0, 0):
                    x2_deps += ["s15_ev0", "s15_ev1", "s14_ev0", "s14_ev1"]
                else:
                    x2_deps += [prevf["hid0"], prevf["hid1"]]
                tstep(em, deps=x2_deps, event=f"f{g10}_{lht}_x2")

                for h, psx1, psx2 in ((0, p0, p2), (1, p1, p3)):
                    sgb = wstg[:, (lht % 2) * 2048 + h * 512:(lht % 2) * 2048 + (h + 1) * 512]
                    sstep(lambda e, psx1=psx1, sgb=sgb: e.activation(out=sgb, in_=psx1[:, :],
                                                                    func=AF.Sigmoid),
                          deps=[f"f{g10}_{lht}_x1"], event=f"f{g10}_{lht}_sig{h}")
                    vstep(lambda e, psx2=psx2, sgb=sgb, lht=lht, h=h: e.tensor_tensor(
                              out=hid[:, lht * T + h * 512:lht * T + (h + 1) * 512],
                              in0=psx2[:, :], in1=sgb, op=ALU.mult),
                          deps=[f"f{g10}_{lht}_sig{h}", f"f{g10}_{lht}_x2"] +
                               (["dbg_hid"] if debug and g10 == 1 and lht == 0 else []),
                          event=f"f{g10}_{lht}_hid{h}")
                prevf = {"sig0": f"f{g10}_{lht}_sig0", "sig1": f"f{g10}_{lht}_sig1",
                         "hid0": f"f{g10}_{lht}_hid0", "hid1": f"f{g10}_{lht}_hid1"}
            hid_all = [f"f{g10}_{lht}_hid{h}" for lht in range(20, 22) for h in range(2)]
            if debug and g10 == 0:
                dma("gpsimd", "gd", dbg["hid"], hid[:, :],
                    deps=[f"f0_{lht}_hid{h}" for lht in range(22) for h in range(2)],
                    event="dbg_hid")

            for mt in range(16):
                w3s = fstg[:, (mt % 2) * 2816:(mt % 2 + 1) * 2816]
                st_deps = ["AG_f3", f"f{g10}_21_x2", f"f{g10}_21_x1"]
                if mt >= 2:
                    st_deps.append(f"o{g10}_{mt-2}_mm")
                dma("sync", f"sw{mt % 2}", w3s,
                    agout_f3[mt // 2, mt % 2, :, g10 * 2816:(g10 + 1) * 2816],
                    deps=st_deps, event=f"o{g10}_{mt}_w")

                def em(e, w3s=w3s):
                    for lht in range(22):
                        e.matmul(out=p4[:, :], lhsT=w3s[:, lht * 128:(lht + 1) * 128],
                                 rhs=hid[:, lht * T:lht * T + 512], start=(lht == 0),
                                 stop=(lht == 21))
                    for lht in range(22):
                        e.matmul(out=p5[:, :], lhsT=w3s[:, lht * 128:(lht + 1) * 128],
                                 rhs=hid[:, lht * T + 512:lht * T + 1024], start=(lht == 0),
                                 stop=(lht == 21))
                    return e.drain()
                mm_deps = [f"o{g10}_{mt}_w"] + hid_all
                if g10 == 0 and mt == 0:
                    mm_deps += ["v15_1_vev", "v15_0_vev", "qq15_rbe1"]
                if mt >= 1:
                    mm_deps += [f"o{g10}_{mt-1}_pf0", f"o{g10}_{mt-1}_pf1"]
                tstep(em, deps=mm_deps, event=f"o{g10}_{mt}_mm")

                if g10 == 0:
                    for h, ps in ((0, p4), (1, p5)):
                        sstep(lambda e, ps=ps, mt=mt, h=h: e.copy(
                                  out=kcv[:, mt * T + h * 512:mt * T + (h + 1) * 512],
                                  in_=ps[:, :]),
                              deps=[f"o0_{mt}_mm"], event=f"o0_{mt}_pf{h}")
                else:
                    for h, ps in ((0, p4), (1, p5)):
                        fs_b = wstg[:, (mt % 2) * 2048 + h * 512:(mt % 2) * 2048 + (h + 1) * 512]
                        sgm_b = wstg[:, (mt % 2) * 2048 + 1024 + h * 512:
                                     (mt % 2) * 2048 + 1024 + (h + 1) * 512]
                        vdeps = [f"o1_{mt}_mm", f"o0_{mt}_pf{h}"]
                        if mt >= 2:
                            vdeps.append(f"o1_{mt-2}_sg{h}")
                        vstep(lambda e, ps=ps, fs_b=fs_b, mt=mt, h=h: e.tensor_tensor(
                                  out=fs_b, in0=ps[:, :],
                                  in1=kcv[:, mt * T + h * 512:mt * T + (h + 1) * 512],
                                  op=ALU.add),
                              deps=vdeps, event=f"o1_{mt}_pf{h}")
                        sstep(lambda e, fs_b=fs_b, sgm_b=sgm_b: e.activation(
                                  out=sgm_b, in_=fs_b, func=AF.Sigmoid),
                              deps=[f"o1_{mt}_pf{h}"], event=f"o1_{mt}_si{h}")

                        def em(e, fs_b=fs_b, sgm_b=sgm_b, mt=mt, h=h):
                            sl = slice(mt * T + h * 512, mt * T + (h + 1) * 512)
                            e.tensor_tensor(out=fs_b, in0=fs_b, in1=sgm_b, op=ALU.mult)
                            return e.tensor_tensor(out=qhs[:, sl], in0=qhs[:, sl], in1=fs_b,
                                                   op=ALU.mult)
                        vstep(em, deps=[f"o1_{mt}_si{h}", f"s{mt}_ev{h}"] +
                              (["dbg_qhs"] if debug and mt == 0 else []),
                              event=f"o1_{mt}_sg{h}")
        sg_all = [f"o1_{mt}_sg{h}" for mt in range(16) for h in range(2)]

        for i in range(8):
            xs = xarena[:, 0:2048]
            os_ = xarena[:, 2048:4096]
            ld_deps = sg_all if i == 0 else [f"out{i-1}"]
            dma("sync", "sx2", xs, x_in[i * 128:(i + 1) * 128, :],
                deps=ld_deps, event=f"fx{i}")
            pairs = []
            for mt in range(16):
                pairs.append((qhs[:, mt * T + i * 128:mt * T + (i + 1) * 128],
                              (xs[:, mt * 128:(mt + 1) * 128], os_[:, mt * 128:(mt + 1) * 128])))

            def vadd(e, banktile, dstp):
                xsl, osl = dstp
                return e.tensor_tensor(out=osl, in0=xsl, in1=banktile, op=ALU.add)
            ft_first = (sg_all + ["o1_15_mm"]) if i == 0 else []
            evs = transpose_tiles(pairs, f"fin{i}", first_tdeps=ft_first, vemit=vadd,
                                  vdeps_extra=[[f"fx{i}"] + ([f"out{i-1}"] if i > 0 and g == 0 else [])
                                               for g in range(4)])
            pr.events[f"fo{i}_15"] = pr.events[evs[3]]
            dma("sync", "sy", y_out[i * 128:(i + 1) * 128, :], os_,
                deps=[f"fo{i}_15"], event=f"out{i}")

        if stop_after is None:
            final_deps = [f"out{i}" for i in range(8)]
            if debug:
                final_deps += ["dbg_xn", "dbg_kcv", "dbg_sc", "dbg_carry", "dbg_thr",
                               "dbg_qhs", "dbg_hid", "dbg_qw"]
            pr.step("sync", lambda e: e.nop(), deps=final_deps)
            stop_idx = None
        else:
            stop_idx = pr.events[stop_after][3]
            pr.steps.append(dict(eng="sync", emit=lambda e: e.nop(),
                                 deps=[stop_after], event=None, inc=None, inc_amt=0,
                                 idx=stop_idx))

        for eng_name, section in [("sync", block.sync), ("scalar", block.scalar),
                                  ("vector", block.vector), ("tensor", block.tensor),
                                  ("gpsimd", block.gpsimd)]:
            def mk(eng_name=eng_name):
                def _(e):
                    pr.emit_engine(eng_name, e, sems, stop_idx=stop_idx)
                return _
            section(mk())

    return nc, dbg



def make_consts():
    m = np.arange(128)
    ang = 2 * np.pi * np.outer(m, m) / 128.0
    consts = np.concatenate([np.cos(ang), -np.sin(ang), np.cos(ang) / 128.0,
                             -np.sin(ang) / 128.0, np.eye(128)], axis=1)
    return consts.astype(ml_dtypes.bfloat16)


def make_inputs(full, c):
    x = np.ascontiguousarray(np.asarray(full["L0_attn_x"], np.float32).reshape(8192, D)[c * T:(c + 1) * T])
    mask = np.zeros((128, 10), np.float32)
    mask[:, :c] = 1.0
    mask[:, 8] = 1e-5
    cs = lambda a: np.ascontiguousarray(np.asarray(a, np.float32))
    return {
        "x_sh": x,
        "wq_sh": cs(full["L0_attn_W_q"][c * 256:(c + 1) * 256]),
        "wkv_sh": cs(full["L0_attn_W_kv"][c * 256:(c + 1) * 256]),
        "wo_sh": cs(full["L0_attn_W_o"][c * 256:(c + 1) * 256]),
        "w1_sh": cs(full["L0_ffn_w1"][:, c * 256:(c + 1) * 256]),
        "w2_sh": cs(full["L0_ffn_w2"][:, c * 256:(c + 1) * 256]),
        "w3_sh": cs(full["L0_ffn_w3"][c * 256:(c + 1) * 256]),
        "consts": make_consts(),
        "onesc_i": np.ones((128, 1), ml_dtypes.bfloat16),
        "onesr_i": np.ones((1, 128), ml_dtypes.bfloat16),
        "mask_i": mask,
    }


def kernel(**inputs):
    nc, _ = build_nc(debug=False)
    in_maps = [make_inputs(inputs, c) for c in range(8)]
    res = run_bass_kernel_spmd(nc, in_maps, core_ids=list(range(8)))
    y = np.concatenate([res.results[c]["y_sh"] for c in range(8)], axis=0)
    return y.reshape(1, 8192, D).astype(np.float32)
